# version 16
# baseline (speedup 1.0000x reference)
import os as _os
import numpy as np
import ml_dtypes
import concourse.bass as bass
import concourse.mybir as mybir
from concourse.bass_utils import run_bass_kernel_spmd

F32 = mybir.dt.float32
BF16 = mybir.dt.bfloat16
AF = mybir.ActivationFunctionType
ALU = mybir.AluOpType
AX = mybir.AxisListType

D = 2048
NF = 16
TT = 512
NTB = 4
NTILES = 8
NE = 8
SCALE_ATT = float(128 ** (-0.5))
EPS_LN = 1e-5
EPS_ATT = 1e-9

SAME_ENGINE_SYNC = False
EXP_NH = int(_os.environ.get("EXP_NH", "16"))
SKIP_ROUTER = bool(int(_os.environ.get("SKIP_ROUTER", "0")))


class Prog:

    ENGS = ("pe", "dve", "act", "sp", "pool")
    DMA_SLOTS = {"sp": 48, "act": 16, "pool": 16}

    def __init__(self):
        self.items = {e: [] for e in self.ENGS}
        self.cnt = {e: 0 for e in self.ENGS}
        self.dma_cnt = {}
        self.dma_rr = {q: 0 for q in self.DMA_SLOTS}
        for q, n in self.DMA_SLOTS.items():
            for k in range(n):
                self.dma_cnt[f"{q}_d{k}"] = 0
        self.know = {e: {} for e in self.ENGS}
        self.snap = {}
        self.last_w = {}
        self.readers = {}
        self.max_wait = {}

    SELF_SYNC = ("dve", "act", "pool")

    def _wait(self, eng, sem, val):
        if sem == eng and eng not in self.SELF_SYNC:
            return
        k = self.know[eng]
        if k.get(sem, 0) >= val:
            return
        if sem in self.cnt:
            assert val <= self.cnt[sem], (
                f"wait on {sem}>={val} but only {self.cnt[sem]} incs emitted "
                f"(consumer of a non-inc'd op before its group's inc)")
        self.items[eng].append(("wait", sem, val))
        if sem == "pe" and eng in ("dve", "act"):
            self.items[eng].append(("spacer",))
        self.max_wait[sem] = max(self.max_wait.get(sem, 0), val)
        sn = self.snap.get((sem, val))
        if sn:
            for s2, v2 in sn.items():
                if k.get(s2, 0) < v2:
                    k[s2] = v2
        k[sem] = val

    FULL_SERIAL = bool(int(_os.environ.get("FULL_SERIAL", "0")))

    def op(self, eng, fn, reads=(), writes=(), inc=True, dma=False):
        assert eng in self.ENGS
        if self.FULL_SERIAL:
            for s_ in self.ENGS:
                if s_ != eng and self.cnt[s_] > 0:
                    self._wait(eng, s_, self.cnt[s_])
            for s_, v_ in self.dma_cnt.items():
                if v_ > 0:
                    self._wait(eng, s_, v_)
        if dma:
            k = self.dma_rr[eng] % self.DMA_SLOTS[eng]
            self.dma_rr[eng] += 1
            sem = f"{eng}_d{k}"
            self.dma_cnt[sem] += 16
            ident = (sem, self.dma_cnt[sem])
        else:
            sem = eng
            if inc:
                self.cnt[eng] += 1
                ident = (eng, self.cnt[eng])
            else:
                ident = (eng, self.cnt[eng] + 1)
        for u_ in reads:
            lw = self.last_w.get(u_)
            if lw is not None:
                self._wait(eng, lw[0], lw[1])
            if u_.startswith("ps"):
                for s, v in self.readers.get(u_, {}).items():
                    self._wait(eng, s, v)
        for u_ in writes:
            lw = self.last_w.get(u_)
            if lw is not None:
                self._wait(eng, lw[0], lw[1])
            for s, v in self.readers.get(u_, {}).items():
                self._wait(eng, s, v)
        self.snap[ident] = dict(self.know[eng])
        for u_ in writes:
            self.last_w[u_] = ident
            self.readers[u_] = {}
        for u_ in reads:
            rd = self.readers.setdefault(u_, {})
            if rd.get(ident[0], 0) < ident[1]:
                rd[ident[0]] = ident[1]
        if dma:
            self.items[eng].append(("op", fn, sem, 16))
        else:
            self.items[eng].append(("op", fn, sem, 1 if inc else None))

    def barrier_op(self, eng, fn):
        self.items[eng].append(("op", fn, None, None))

    def finalize_check(self):
        for sem, v in self.max_wait.items():
            have = self.dma_cnt.get(sem, self.cnt.get(sem, 0))
            assert have >= v, f"deadlock: sem {sem} max wait {v} > total {have}"

    import os
    PE_DRAIN_NOP = int(os.environ.get("PE_DRAIN_NOP", "160"))

    def emit(self, eng_name, eng_obj, sems, spacer=None):
        for it in self.items[eng_name]:
            if it[0] == "spacer":
                if spacer is not None:
                    spacer(eng_obj)
                continue
            if it[0] == "wait":
                _, s, v = it
                eng_obj.wait_ge(sems[s], v)
            else:
                _, fn, sem, inc = it
                inst = fn(eng_obj)
                if sem is not None and inc is not None:
                    inst.then_inc(sems[sem], inc)


def build(n_tiles=NTILES, taps=()):
    taps = set(taps)
    nc = bass.Bass("TRN2", target_bir_lowering=False)

    x_in = nc.dram_tensor("x_in", [2, NF, 128, D], F32, kind="ExternalInput").ap()
    hq = nc.dram_tensor("hq", [48, 128, 16, 128], F32, kind="ExternalInput").ap()
    hw = nc.dram_tensor("hw", [16, 128, 16, 128], F32, kind="ExternalInput").ap()
    ho = nc.dram_tensor("ho", [16, 128, 16, 128], F32, kind="ExternalInput").ap()
    he = nc.dram_tensor("he", [8, 2, 2, 128, 8192], F32, kind="ExternalInput").ap()
    rw_in = nc.dram_tensor("rw_in", [128, 128], F32, kind="ExternalInput").ap()
    rb_in = nc.dram_tensor("rb_in", [1, 8], F32, kind="ExternalInput").ap()
    eb_in = nc.dram_tensor("eb_in", [1, 16384], F32, kind="ExternalInput").ap()
    ob_in = nc.dram_tensor("ob_in", [128, 16], F32, kind="ExternalInput").ap()
    idn_in = nc.dram_tensor("idn_in", [128, 128], BF16, kind="ExternalInput").ap()
    onesc_in = nc.dram_tensor("onesc_in", [128, 1], BF16, kind="ExternalInput").ap()
    onesr_in = nc.dram_tensor("onesr_in", [1, 128], BF16, kind="ExternalInput").ap()
    onesrf_in = nc.dram_tensor("onesrf_in", [1, 128], F32, kind="ExternalInput").ap()
    out = nc.dram_tensor("out", [2, NF, 128, D], F32, kind="ExternalOutput").ap()
    dbg = {}
    for t_ in taps:
        dt_ = F32 if t_ in ("x_att", "gates", "lgts") else BF16
        cols_ = {"gates": 32, "lgts": 32, "h0": 2048, "z0": 1024, "z1": 1024, "eb0": 2048}.get(t_, NF * TT)
        if t_ in ("z0", "z1"): dt_ = F32
        dbg[t_] = nc.dram_tensor(f"dbg_{t_}", [128, cols_], dt_, kind="ExternalOutput").ap()
    sq = nc.dram_tensor("sq", [48, 128, 2048], BF16, kind="Internal").ap()
    sw = nc.dram_tensor("sw", [16, 128, 2048], BF16, kind="Internal").ap()
    so = nc.dram_tensor("so", [16, 128, 2048], BF16, kind="Internal").ap()
    se = nc.dram_tensor("se", [8, 2, 2, 128, 8192], BF16, kind="Internal").ap()
    seb = nc.dram_tensor("seb", [1, 16384], BF16, kind="Internal").ap()

    p = Prog()

    from contextlib import ExitStack
    with ExitStack() as _st:
        xb = _st.enter_context(nc.sbuf_tensor("xb", [128, NF * TT], F32))
        yb = _st.enter_context(nc.sbuf_tensor("yb", [128, NF * TT], BF16))
        kb = _st.enter_context(nc.sbuf_tensor("kb", [128, NF * TT], BF16))
        vb = _st.enter_context(nc.sbuf_tensor("vb", [128, NF * TT], BF16))
        kvcb = _st.enter_context(nc.sbuf_tensor("kvcb", [128, NF * TT], BF16))
        kcb = _st.enter_context(nc.sbuf_tensor("kcb", [128, NF * TT], BF16))
        wq0 = _st.enter_context(nc.sbuf_tensor("wq0", [128, 2048], BF16))
        wq1 = _st.enter_context(nc.sbuf_tensor("wq1", [128, 2048], BF16))
        wq2 = _st.enter_context(nc.sbuf_tensor("wq2", [128, 2048], BF16))
        wq3 = _st.enter_context(nc.sbuf_tensor("wq3", [128, 2048], BF16))
        we0 = _st.enter_context(nc.sbuf_tensor("we0", [128, 8192], BF16))
        we1 = _st.enter_context(nc.sbuf_tensor("we1", [128, 8192], BF16))
        we2 = _st.enter_context(nc.sbuf_tensor("we2", [128, 8192], BF16))
        kvsc = _st.enter_context(nc.sbuf_tensor("kvsc", [128, 2 * TT], BF16))
        fsc = _st.enter_context(nc.sbuf_tensor("fsc", [128, 3 * TT], F32))
        bsc = _st.enter_context(nc.sbuf_tensor("bsc", [128, 6 * TT], BF16))
        hsc = _st.enter_context(nc.sbuf_tensor("hsc", [128, 2 * 1024], BF16))
        rows = _st.enter_context(nc.sbuf_tensor("rows", [1, 1536], F32))
        kvst = _st.enter_context(nc.sbuf_tensor("kvst", [128, NF], F32))
        kcst = _st.enter_context(nc.sbuf_tensor("kcst", [128, NF], F32))
        gts = _st.enter_context(nc.sbuf_tensor("gts", [128, 32], F32))
        lgb = _st.enter_context(nc.sbuf_tensor("lgb", [128, 32], F32)) if "lgts" in taps else None
        rsc = _st.enter_context(nc.sbuf_tensor("rsc", [128, 40], F32))
        consts = _st.enter_context(nc.sbuf_tensor("consts", [128, 4], F32))
        junk = _st.enter_context(nc.sbuf_tensor("junk", [128, 256], F32))
        rwb = _st.enter_context(nc.sbuf_tensor("rwb", [128, 128], BF16))
        rbb = _st.enter_context(nc.sbuf_tensor("rbb", [1, 8], BF16))
        ebs = _st.enter_context(nc.sbuf_tensor("ebs", [1, 2048], BF16))
        obb = _st.enter_context(nc.sbuf_tensor("obb", [128, 16], F32))
        idn = _st.enter_context(nc.sbuf_tensor("idn", [128, 128], BF16))
        onesc = _st.enter_context(nc.sbuf_tensor("onesc", [128, 1], BF16))
        onesr = _st.enter_context(nc.sbuf_tensor("onesr", [1, 128], BF16))
        onesrf = _st.enter_context(nc.sbuf_tensor("onesrf", [1, 128], F32))
        psA0 = _st.enter_context(nc.psum_tensor("psA0", [128, 512], F32))
        psA1 = _st.enter_context(nc.psum_tensor("psA1", [128, 512], F32))
        psA2 = _st.enter_context(nc.psum_tensor("psA2", [128, 512], F32))
        psA3 = _st.enter_context(nc.psum_tensor("psA3", [128, 512], F32))
        psB0 = _st.enter_context(nc.psum_tensor("psB0", [128, 512], F32))
        psB1 = _st.enter_context(nc.psum_tensor("psB1", [128, 512], F32))
        psB2 = _st.enter_context(nc.psum_tensor("psB2", [128, 512], F32))
        psB3 = _st.enter_context(nc.psum_tensor("psB3", [128, 512], F32))
        sem_pe = _st.enter_context(nc.semaphore("pe"))
        sem_dve = _st.enter_context(nc.semaphore("dve"))
        sem_act = _st.enter_context(nc.semaphore("act"))
        sems = {"pe": sem_pe, "dve": sem_dve, "act": sem_act}
        for q_, n_ in Prog.DMA_SLOTS.items():
            for k_ in range(n_):
                nm = f"{q_}_d{k_}"
                sems[nm] = _st.enter_context(nc.semaphore(nm))
        block = _st.enter_context(nc.Block())
        nums = sorted(s.num for s in sems.values())
        assert nums == list(range(nums[0], nums[0] + len(nums))), nums
        srange = range(nums[0], nums[-1] + 1)
        nc.gpsimd.dma_reset(srange)
        nc.gpsimd.sem_clear(srange)
        nc.all_engine_barrier()
        psA = [psA0, psA1, psA2, psA3]
        psB = [psB0, psB1, psB2, psB3]
        wq = [wq0, wq1, wq2, wq3]
        we = [we0, we1, we2]

        def u(buf, i, w=TT):
            return buf[:, i * w:(i + 1) * w]

        p.op("dve", lambda v: v.memset(consts[:, 0:1], SCALE_ATT), writes=["c0"])
        p.op("dve", lambda v: v.memset(consts[:, 1:2], EPS_LN), writes=["c1"])
        p.op("dve", lambda v: v.memset(consts[:, 2:3], EPS_ATT), writes=["c2"])
        c_att = consts[:, 0:1]
        c_eps5 = consts[:, 1:2]
        c_eps9 = consts[:, 2:3]
        p.op("sp", lambda s: s.dma_start(idn[:, :], idn_in), writes=["idn"], dma=True)
        p.op("sp", lambda s: s.dma_start(onesc[:, :], onesc_in), writes=["onesc"], dma=True)
        p.op("sp", lambda s: s.dma_start(onesr[:, :], onesr_in), writes=["onesr"], dma=True)
        p.op("sp", lambda s: s.dma_start(onesrf[:, :], onesrf_in), writes=["onesrf"], dma=True)
        p.op("pool", lambda g: g.dma_start(rwb[:, :], rw_in), writes=["rw"], dma=True)
        p.op("pool", lambda g: g.dma_start(rbb[:, :], rb_in), writes=["rb"], dma=True)
        p.op("sp", lambda s: s.dma_start(obb[:, :], ob_in), writes=["ob"], dma=True)

        wq_state = {}

        def load_wq_slab(tile_i, kind, idx, slot):
            dst = wq[slot]
            key = f"wqs:{slot}"
            if kind == "q":
                src_f, src_b, skey = hq[idx], sq[idx], f"sq:{idx}"
            elif kind == "w":
                src_f, src_b, skey = hw[idx], sw[idx], f"sw:{idx}"
            else:
                src_f, src_b, skey = ho[idx], so[idx], f"so:{idx}"
            if tile_i == 0:
                sf = src_f.rearrange("p a b -> p (a b)")
                p.op("pool", lambda g, d=dst, s=sf: g.dma_start(d[:, :], s),
                     writes=[key], dma=True)
                p.op("act", lambda a, d=dst, s=src_b: a.dma_start(s, d[:, :]),
                     reads=[key], writes=[skey], dma=True)
            else:
                p.op("sp", lambda s_, d=dst, s=src_b: s_.dma_start(d[:, :], s),
                     reads=[skey], writes=[key], dma=True)
            return dst, key

        def load_we_slab(tile_i, e, dh, hh, slot):
            dst = we[slot]
            key = f"wes:{slot}"
            skey = f"se:{e}:{dh}:{hh}"
            if tile_i == 0:
                sf = he[e, dh, hh]
                p.op("pool", lambda g, d=dst, s=sf: g.dma_start(d[:, :], s),
                     writes=[key], dma=True)
                p.op("act", lambda a, d=dst, s=se[e, dh, hh]: a.dma_start(s, d[:, :]),
                     reads=[key], writes=[skey], dma=True)
            else:
                p.op("sp", lambda s_, d=dst, s=se[e, dh, hh]: s_.dma_start(d[:, :], s),
                     reads=[skey], writes=[key], dma=True)
            return dst, key

        def layernorm(t_i, tag, post_chunk=None):
            xkeys = [f"x:{fc}" for fc in range(NF)]
            for fc in range(NF):
                sb = u(bsc, fc % 2)
                sq_ = u(bsc, 2 + fc % 2)
                p.op("dve", lambda v, a=sb, b=u(xb, fc): v.tensor_copy(a, b),
                     reads=[f"x:{fc}"], writes=[f"bsc:{fc % 2}"])
                p.op("act", lambda a_, a=sq_, b=u(xb, fc): a_.activation(a, b, AF.Square),
                     reads=[f"x:{fc}"], writes=[f"bsc:{2 + fc % 2}"])
                p.op("pe", lambda t, a=sb, fc=fc: t.matmul(
                    psA0[0:1, :], onesc[:, :], a, start=(fc == 0), stop=(fc == NF - 1)),
                    reads=[f"bsc:{fc % 2}", "onesc"], writes=["psA0"])
                p.op("pe", lambda t, a=sq_, fc=fc: t.matmul(
                    psA1[0:1, :], onesc[:, :], a, start=(fc == 0), stop=(fc == NF - 1)),
                    reads=[f"bsc:{2 + fc % 2}", "onesc"], writes=["psA1"])
            m_r = rows[0:1, 0:512]
            r_r = rows[0:1, 512:1024]
            t_r = rows[0:1, 1024:1536]
            p.op("dve", lambda v: v.tensor_scalar(m_r, psA0[0:1, :], 1.0 / D, None, ALU.mult),
                 reads=["psA0"], writes=["mr"])
            p.op("dve", lambda v: v.tensor_mul(t_r, m_r, m_r), reads=["mr"], writes=["tr"])
            p.op("dve", lambda v: v.scalar_tensor_tensor(
                t_r, psA1[0:1, :], 1.0 / D, t_r, ALU.mult, ALU.subtract),
                reads=["psA1", "tr"], writes=["tr"])
            p.op("act", lambda a: a.activation(t_r, t_r, AF.Ln, bias=c_eps5[0:1, 0:1]),
                 reads=["tr", "c1"], writes=["tr"])
            p.op("act", lambda a: a.activation(r_r, t_r, AF.Exp, scale=-0.5),
                 reads=["tr"], writes=["rr"])
            p.op("pe", lambda t: t.matmul(psA2[:, :], onesrf[:, :], m_r, start=True, stop=True),
                 reads=["mr", "onesrf"], writes=["psA2"])
            p.op("pe", lambda t: t.matmul(psA3[:, :], onesrf[:, :], r_r, start=True, stop=True),
                 reads=["rr", "onesrf"], writes=["psA3"])
            for fc in range(NF):
                sf = u(fsc, fc % 2)
                p.op("dve", lambda v, a=sf, b=u(xb, fc): v.tensor_sub(a, b, psA2[:, :]),
                     reads=[f"x:{fc}", "psA2"], writes=[f"fsc:{fc % 2}"])
                p.op("dve", lambda v, a=u(yb, fc), b=sf: v.tensor_mul(a, b, psA3[:, :]),
                     reads=[f"fsc:{fc % 2}", "psA3"], writes=[f"y:{fc}"])
                if post_chunk is not None:
                    post_chunk(fc)


        def tap(name, buf, i):
            if i == 0 and name in dbg:
                keys = [f"{pfx}:{u_}" for pfx in [ {"y1":"y","y2":"y","k":"k","v":"v",
                        "kc":"kc","kvc":"kvc","attn":"v","comb_tm":"kvc","comb_fm":"k"}.get(name, name) ]
                        for u_ in range(NF)]
                if name == "gates":
                    keys = [f"g:{tb_}" for tb_ in range(NTB)]
                elif name == "x_att":
                    keys = [f"x:{u_}" for u_ in range(NF)]
                p.op("sp", lambda s, b_=buf, d_=dbg[name]: s.dma_start(d_, b_[:, :]),
                     reads=keys, dma=True)

        for i in range(n_tiles):
            b, l0 = i // 4, (i % 4) * TT
            seq_start = (i % 4 == 0)

            for fc in range(NF):
                p.op("pool", lambda g, fc=fc, b=b, l0=l0: g.dma_start(
                    u(xb, fc), x_in[b, fc, :, l0:l0 + TT]),
                    writes=[f"x:{fc}"], dma=True)

            if seq_start:
                p.op("dve", lambda v: v.memset(kvst[:, :], 0.0), writes=["kvst"])
                p.op("dve", lambda v: v.memset(kcst[:, :], 0.0), writes=["kcst"])

            layernorm(i, "ln1")
            ykeys = [f"y:{fc}" for fc in range(NF)]

            tap("y1", yb, i)

            def proj_chunk(oc, slot, bank, stop_inc=True):
                dst, wkey = load_wq_slab(i, "q", oc, slot)
                for fc in range(NF):
                    p.op("pe", lambda t, w_=dst, fc=fc, bank=bank: t.matmul(
                        psA[bank][:, :], w_[:, fc * 128:(fc + 1) * 128], u(yb, fc),
                        start=(fc == 0), stop=(fc == NF - 1)),
                        reads=[wkey, f"y:{fc}"], writes=[f"psA{bank}"],
                        inc=(fc == NF - 1))

            def phi_from_psum(bank, out_ap, out_key, sidx):
                mn = u(fsc, 2)
                ex = u(bsc, 4 + sidx % 2)
                p.op("act", lambda a_, a=mn, bank=bank: a_.activation(
                    a, psA[bank][:, :], AF.Relu, scale=-1.0),
                    reads=[f"psA{bank}"], writes=["fsc:2"])
                p.op("act", lambda a_, a=ex, b=mn: a_.activation(a, b, AF.Exp, scale=-1.0),
                     reads=["fsc:2"], writes=[f"bsc:{4 + sidx % 2}"])
                p.op("dve", lambda v, o=out_ap, e=ex, bank=bank: v.scalar_tensor_tensor(
                    o, psA[bank][:, :], 0.0, e, ALU.max, ALU.add),
                    reads=[f"psA{bank}", f"bsc:{4 + sidx % 2}"], writes=[out_key])

            for j in range(NF):
                oc = 16 + j
                bank = j % 4
                proj_chunk(oc, j % 4, bank)
                phi_from_psum(bank, u(kb, j), f"k:{j}", j)
                p.op("dve", lambda v, j=j: v.tensor_tensor_scan(
                    u(kcb, j), u(kb, j), u(kb, j), kcst[:, j:j + 1], ALU.add, ALU.bypass),
                    reads=[f"k:{j}", "kcst"], writes=[f"kc:{j}"])
                p.op("dve", lambda v, j=j: v.tensor_copy(
                    kcst[:, j:j + 1], kcb[:, j * TT + TT - 1:j * TT + TT]),
                    reads=[f"kc:{j}"], writes=["kcst"])
            for j in range(NF):
                oc = 32 + j
                bank = j % 4
                proj_chunk(oc, j % 4, bank)
                p.op("dve", lambda v, j=j, bank=bank: v.tensor_copy(
                    u(vb, j), psA[bank][:, :]),
                    reads=[f"psA{bank}"], writes=[f"v:{j}"])
                kvs = u(kvsc, j % 2)
                p.op("dve", lambda v, a=kvs, j=j: v.tensor_mul(a, u(kb, j), u(vb, j)),
                     reads=[f"k:{j}", f"v:{j}"], writes=[f"kvsc:{j % 2}"])
                p.op("dve", lambda v, a=kvs, j=j: v.tensor_tensor_scan(
                    u(kvcb, j), a, a, kvst[:, j:j + 1], ALU.add, ALU.bypass),
                    reads=[f"kvsc:{j % 2}", "kvst"], writes=[f"kvc:{j}"])
                p.op("dve", lambda v, j=j: v.tensor_copy(
                    kvst[:, j:j + 1], kvcb[:, j * TT + TT - 1:j * TT + TT]),
                    reads=[f"kvc:{j}"], writes=["kvst"])
            for j in range(NF):
                oc = j
                bank = j % 4
                proj_chunk(oc, j % 4, bank)
                qap = u(bsc, j % 2)
                phi_from_psum(bank, qap, f"bsc:{j % 2}", j)
                den = u(fsc, j % 2)
                p.op("dve", lambda v, a=den, q=qap, j=j: v.tensor_mul(a, q, u(kcb, j)),
                     reads=[f"bsc:{j % 2}", f"kc:{j}"], writes=[f"fsc:{j % 2}"])
                p.op("act", lambda a_, a=den: a_.activation(a, a, AF.Ln, bias=c_eps9),
                     reads=[f"fsc:{j % 2}", "c2"], writes=[f"fsc:{j % 2}"])
                rcp = u(bsc, 2 + j % 2)
                p.op("act", lambda a_, a=rcp, b=den: a_.activation(a, b, AF.Exp, scale=-1.0),
                     reads=[f"fsc:{j % 2}"], writes=[f"bsc:{2 + j % 2}"])
                num = u(kvsc, j % 2)
                p.op("dve", lambda v, a=num, q=qap, j=j: v.tensor_mul(a, q, u(kvcb, j)),
                     reads=[f"bsc:{j % 2}", f"kvc:{j}"], writes=[f"kvsc:{j % 2}"])
                p.op("dve", lambda v, a=u(vb, j), n=num, r=rcp: v.tensor_mul(a, n, r),
                     reads=[f"kvsc:{j % 2}", f"bsc:{2 + j % 2}"], writes=[f"v:{j}"])

            tap("k", kb, i)
            tap("v", vb, i)
            tap("kc", kcb, i)
            tap("kvc", kvcb, i)
            tap("attn", vb, i)

            for fo in range(NF):
                bank = fo % 4
                dst, wkey = load_wq_slab(i, "w", fo, fo % 4)
                for dc in range(NF):
                    p.op("pe", lambda t, w_=dst, dc=dc, bank=bank: t.matmul(
                        psA[bank][:, :], w_[:, dc * 128:(dc + 1) * 128], u(vb, dc),
                        start=(dc == 0), stop=(dc == NF - 1)),
                        reads=[wkey, f"v:{dc}"], writes=[f"psA{bank}"],
                        inc=(dc == NF - 1))
                p.op("dve", lambda v, fo=fo, bank=bank: v.scalar_tensor_tensor(
                    u(xb, fo), psA[bank][:, :], c_att, u(xb, fo), ALU.mult, ALU.add),
                    reads=[f"psA{bank}", f"x:{fo}", "c0"], writes=[f"x:{fo}"])

            tap("x_att", xb, i)

            def router_chunk(fc):
                for tb in range(NTB):
                    lgp = psA0[:, tb * 8:(tb + 1) * 8]
                    p.op("pe", lambda t, fc=fc, tb=tb, lgp=lgp: t.matmul(
                        lgp, yb[:, fc * TT + tb * 128: fc * TT + tb * 128 + 128],
                        rwb[:, fc * 8:(fc + 1) * 8],
                        start=(fc == 0 and tb == 0), stop=False),
                        reads=[f"y:{fc}", "rw"], writes=["psA0"], inc=False)

            layernorm(i, "ln2", post_chunk=None if SKIP_ROUTER else router_chunk)

            if SKIP_ROUTER:
                p.op("dve", lambda v: v.memset(gts[:, :], 0.25),
                     writes=[f"g:{tb_}" for tb_ in range(NTB)])
            for tb in range(NTB if not SKIP_ROUTER else 0):
                lgp = psA0[:, tb * 8:(tb + 1) * 8]
                p.op("pe", lambda t, lgp=lgp: t.matmul(
                    lgp, onesr[:, :], rbb[:, :], start=False, stop=True),
                    reads=["onesr", "rb"], writes=["psA0"])
                lg = rsc[:, 0:8]
                m1 = rsc[:, 8:9]
                m2 = rsc[:, 9:10]
                msk = rsc[:, 16:24]
                eg = rsc[:, 24:32]
                lg2 = rsc[:, 32:40]
                p.op("dve", lambda v, lgp=lgp: v.tensor_copy(lg, lgp),
                     reads=["psA0"], writes=["rsc"])
                if lgb is not None:
                    p.op("dve", lambda v, tb=tb: v.tensor_copy(lgb[:, tb * 8:(tb + 1) * 8], lg),
                         reads=["rsc"], writes=["lgb"])
                p.op("dve", lambda v: v.reduce_max(m1, lg, axis=AX.X),
                     reads=["rsc"], writes=["rsc"])
                p.op("dve", lambda v: v.tensor_scalar(msk, lg, m1, -1e30, ALU.is_ge, ALU.mult),
                     reads=["rsc"], writes=["rsc"])
                p.op("dve", lambda v: v.tensor_add(lg2, lg, msk), reads=["rsc"], writes=["rsc"])
                p.op("dve", lambda v: v.reduce_max(m2, lg2, axis=AX.X),
                     reads=["rsc"], writes=["rsc"])
                p.op("dve", lambda v: v.tensor_scalar(msk, lg, m2, None, ALU.is_ge),
                     reads=["rsc"], writes=["rsc"])
                p.op("dve", lambda v: v.tensor_scalar(m1, m1, -1.0, None, ALU.mult),
                     reads=["rsc"], writes=["rsc"])
                p.op("act", lambda a: a.activation(eg, lg, AF.Exp, bias=m1[:, 0:1]),
                     reads=["rsc"], writes=["rsc"])
                p.op("dve", lambda v: v.tensor_mul(eg, eg, msk), reads=["rsc"], writes=["rsc"])
                p.op("dve", lambda v: v.reduce_sum(m2, eg, axis=AX.X),
                     reads=["rsc"], writes=["rsc"])
                p.op("dve", lambda v: v.reciprocal(m1, m2), reads=["rsc"], writes=["rsc"])
                p.op("dve", lambda v, tb=tb: v.tensor_scalar(
                    gts[:, tb * 8:(tb + 1) * 8], eg, m1, None, ALU.mult),
                    reads=["rsc"], writes=[f"g:{tb}"])

            tap("y2", yb, i)
            tap("gates", gts, i)
            if i == 0 and "lgts" in dbg:
                p.op("sp", lambda s: s.dma_start(dbg["lgts"], lgb[:, :]),
                     reads=["lgb"], dma=True)

            widx = 0
            for e in range(NE):
                if i == 0:
                    p.op("pool", lambda g, e=e: g.dma_start(
                        ebs[:, :], eb_in[0:1, e * 2048:(e + 1) * 2048]),
                        writes=["ebs"], dma=True)
                    p.op("act", lambda a, e=e: a.dma_start(
                        seb[0:1, e * 2048:(e + 1) * 2048], ebs[:, :]),
                        reads=["ebs"], writes=[f"seb:{e}"], dma=True)
                else:
                    p.op("sp", lambda s, e=e: s.dma_start(
                        ebs[:, :], seb[0:1, e * 2048:(e + 1) * 2048]),
                        reads=[f"seb:{e}"], writes=["ebs"], dma=True)
                for dh in range(2):
                    slabs = []
                    for hh in range(2):
                        dst, wkey = load_we_slab(i, e, dh, hh, widx % 3)
                        widx += 1
                        slabs.append((dst, wkey))
                    if i == 0 and e == 0 and dh == 0 and "w0" in dbg:
                        p.op("sp", lambda s: s.dma_start(dbg["w0"], we[0][:, :]),
                             reads=["wes:0"], dma=True)
                    for tb in range(NTB):
                        pb = (e * 8 + dh * 4 + tb) % 2
                        pkey = f"psB{pb}"
                        for hh in range(2):
                            dst, wkey = slabs[hh]
                            for h8 in range(8):
                                h = hh * 8 + h8
                                if h >= EXP_NH:
                                    continue
                                first = (hh == 0 and h8 == 0)
                                for dg in range(2):
                                    p.op("pe", lambda t, d=dst, h=h, h8=h8, dg=dg, tb=tb, pb=pb, first=first: t.matmul(
                                        psB[pb * 2 + dg][:, :],
                                        yb[:, h * TT + tb * 128: h * TT + tb * 128 + 128],
                                        d[:, h8 * 1024 + dg * 512: h8 * 1024 + (dg + 1) * 512],
                                        start=first, stop=False),
                                        reads=[wkey, f"y:{h}"], writes=[f"psB{pb * 2 + dg}"], inc=False)
                        for dg in range(2):
                            off = dh * 1024 + dg * 512
                            p.op("pe", lambda t, dg=dg, pb=pb, off=off: t.matmul(
                                psB[pb * 2 + dg][:, :], onesr[:, :],
                                ebs[0:1, off:off + 512], start=False, stop=True),
                                reads=["onesr", "ebs"], writes=[f"psB{pb * 2 + dg}"],
                                inc=(dg == 1))
                        if i == 0 and e == 0 and dh == 1 and tb == 0 and "z1" in dbg:
                            for dg in range(2):
                                p.op("dve", lambda v, dg=dg, pb=pb: v.tensor_copy(
                                    u(fsc, dg), psB[pb * 2 + dg][:, :]),
                                    reads=[f"psB{pb * 2 + dg}"], writes=[f"fsc:{dg}"])
                                p.op("sp", lambda s, dg=dg: s.dma_start(
                                    dbg["z1"][:, dg * 512:(dg + 1) * 512], u(fsc, dg)),
                                    reads=[f"fsc:{dg}"], dma=True)
                        if i == 0 and e == 0 and dh == 1 and "w0" in dbg and tb == 0:
                            p.op("sp", lambda s, sl=slabs[0][0]: s.dma_start(dbg["w0"], sl[:, :]),
                                 reads=[slabs[0][1]], dma=True)
                        if i == 0 and e == 0 and dh == 0 and tb == 0 and "z0" in dbg:
                            for dg in range(2):
                                p.op("dve", lambda v, dg=dg, pb=pb: v.tensor_copy(
                                    u(fsc, dg), psB[pb * 2 + dg][:, :]),
                                    reads=[f"psB{pb * 2 + dg}"], writes=[f"fsc:{dg}"])
                                p.op("sp", lambda s, dg=dg: s.dma_start(
                                    dbg["z0"][:, dg * 512:(dg + 1) * 512], u(fsc, dg)),
                                    reads=[f"fsc:{dg}"], dma=True)
                        if i == 0 and e == 0 and dh == 0 and tb == 0 and "eb0" in dbg:
                            p.op("sp", lambda s: s.dma_start(dbg["eb0"][0:1, 0:2048], ebs[:, :]),
                                 reads=["ebs"], dma=True)
                        hs = u(hsc, tb % 2, 1024)
                        for dg in range(2):
                            p.op("act", lambda a, h=hs, pb=pb, dg=dg: a.activation(
                                h[:, dg * 512:(dg + 1) * 512], psB[pb * 2 + dg][:, :],
                                AF.Gelu_apprx_tanh),
                                reads=[f"psB{pb * 2 + dg}"], writes=[f"hsc:{tb % 2}"])
                        if i == 0 and e == 0 and tb == 0 and "h0" in dbg:
                            p.op("sp", lambda s, h=hs, dh=dh: s.dma_start(
                                dbg["h0"][:, dh * 1024:(dh + 1) * 1024], h),
                                reads=[f"hsc:{tb % 2}"], dma=True)
                        cunits = [f"kvc:{tb * 4 + dh * 2}", f"kvc:{tb * 4 + dh * 2 + 1}"]
                        cap = kvcb[:, tb * 2048 + dh * 1024: tb * 2048 + (dh + 1) * 1024]
                        gcol = gts[:, tb * 8 + e:tb * 8 + e + 1]
                        if e == 0:
                            p.op("dve", lambda v, c=cap, h=hs, g=gcol: v.tensor_scalar(
                                c, h, g, None, ALU.mult),
                                reads=[f"hsc:{tb % 2}", f"g:{tb}"], writes=cunits)
                        else:
                            p.op("dve", lambda v, c=cap, h=hs, g=gcol: v.scalar_tensor_tensor(
                                c, h, g, c, ALU.mult, ALU.add),
                                reads=[f"hsc:{tb % 2}", f"g:{tb}"] + cunits, writes=cunits)

            tap("comb_tm", kvcb, i)

            for tb in range(NTB):
                pb = tb % 2
                cunits = [f"kvc:{tb * 4 + q_}" for q_ in range(4)]
                for half in range(2):
                    pbt = psB[pb * 2 + half]
                    pbf = pbt[:, :].bitcast(BF16)
                    for d8 in range(8):
                        dc = half * 8 + d8
                        p.op("pe", lambda t, d8=d8, dc=dc, tb=tb, pbf=pbf: t.transpose(
                            pbf[:, d8 * 128:(d8 + 1) * 128],
                            kvcb[:, tb * 2048 + dc * 128: tb * 2048 + dc * 128 + 128],
                            idn[:, :]),
                            reads=cunits + ["idn"], writes=[f"psB{pb * 2 + half}"],
                            inc=(d8 == 7))
                    dst = kb[:, :].rearrange("p (dc t) -> p dc t", dc=NF)[
                        :, half * 8:(half + 1) * 8, tb * 128:tb * 128 + 128]
                    src2 = pbf.rearrange("p (d8 t) -> p d8 t", d8=8)
                    p.op("dve", lambda v, d=dst, s=src2: v.tensor_copy(d, s),
                         reads=[f"psB{pb * 2 + half}"],
                         writes=[f"k:{half * 8 + d8_}" for d8_ in range(8)])

            tap("comb_fm", kb, i)

            for fo in range(NF):
                bank = fo % 4
                dst, wkey = load_wq_slab(i, "o", fo, fo % 4)
                for dc in range(NF):
                    p.op("pe", lambda t, w_=dst, dc=dc, bank=bank: t.matmul(
                        psA[bank][:, :], w_[:, dc * 128:(dc + 1) * 128], u(kb, dc),
                        start=(dc == 0), stop=(dc == NF - 1)),
                        reads=[wkey, f"k:{dc}"], writes=[f"psA{bank}"],
                        inc=(dc == NF - 1))
                p.op("dve", lambda v, fo=fo, bank=bank: v.scalar_tensor_tensor(
                    u(xb, fo), psA[bank][:, :], obb[:, fo:fo + 1], u(xb, fo),
                    ALU.add, ALU.add),
                    reads=[f"psA{bank}", f"x:{fo}", "ob"], writes=[f"x:{fo}"])
                p.op("pool", lambda g, fo=fo, b=b, l0=l0: g.dma_start(
                    out[b, fo, :, l0:l0 + TT], u(xb, fo)),
                    reads=[f"x:{fo}"], dma=True)

        p.finalize_check()
        fc_ = dict(p.dma_cnt)
        cn = dict(p.cnt)

        def emit_all():
            @block.sync
            def _(s):
                p.emit("sp", s, sems)
                for nm_, v_ in fc_.items():
                    if v_ > 0:
                        s.wait_ge(sems[nm_], v_)
                s.wait_ge(sem_pe, cn["pe"])
                s.wait_ge(sem_dve, cn["dve"])
                s.wait_ge(sem_act, cn["act"])

            @block.gpsimd
            def _(g):
                p.emit("pool", g, sems)

            @block.vector
            def _(v):
                p.emit("dve", v, sems,
                       spacer=lambda e: e.tensor_copy(junk[:, 0:128], junk[:, 128:256]))

            @block.scalar
            def _(a):
                p.emit("act", a, sems,
                       spacer=lambda e: e.copy(junk[:, 0:128], junk[:, 128:256]))

            @block.tensor
            def _(t):
                p.emit("pe", t, sems)

        emit_all()
    return nc


def _prep_host(L0_attn_x, L0_attn_W_qkv, L0_attn_W_out, L0_ffn_router_w,
               L0_ffn_router_b, L0_ffn_expert_w, L0_ffn_expert_b,
               L0_ffn_out_w, L0_ffn_out_b):
    bf = ml_dtypes.bfloat16
    f32 = np.float32
    W_qkv = np.asarray(L0_attn_W_qkv, f32)
    W_out = np.asarray(L0_attn_W_out, f32)
    out_w = np.asarray(L0_ffn_out_w, f32)
    exp_w = np.asarray(L0_ffn_expert_w, f32)

    hq = np.ascontiguousarray(W_qkv.reshape(48, 128, 16, 128).transpose(0, 3, 2, 1))
    hw = np.ascontiguousarray(W_out.reshape(16, 128, 16, 128).transpose(0, 3, 2, 1))
    ho = np.ascontiguousarray(out_w.reshape(16, 128, 16, 128).transpose(2, 1, 0, 3))
    he = np.ascontiguousarray(
        exp_w.reshape(8, 2, 8, 128, 2, 1024).transpose(0, 4, 1, 3, 2, 5)
        .reshape(8, 2, 2, 128, 8192))
    rw = np.ascontiguousarray(
        np.asarray(L0_ffn_router_w, f32).reshape(16, 128, 8).transpose(1, 0, 2)
        .reshape(128, 128))
    rb = np.asarray(L0_ffn_router_b, f32).reshape(1, 8)
    eb = np.asarray(L0_ffn_expert_b, f32).reshape(1, 16384)
    ob = np.ascontiguousarray(np.asarray(L0_ffn_out_b, f32).reshape(16, 128).T)
    idn = np.eye(128, dtype=bf)
    onesc = np.ones((128, 1), dtype=bf)
    onesr = np.ones((1, 128), dtype=bf)
    onesrf = np.ones((1, 128), dtype=f32)

    x = np.asarray(L0_attn_x, f32)
    shards = []
    for c in range(8):
        xc = x[2 * c:2 * c + 2]
        xt = np.ascontiguousarray(
            xc.reshape(2, 2048, 16, 128).transpose(0, 2, 3, 1))
        shards.append(xt)
    common = dict(hq=hq, hw=hw, ho=ho, he=he, rw_in=rw, rb_in=rb, eb_in=eb,
                  ob_in=ob, idn_in=idn, onesc_in=onesc, onesr_in=onesr,
                  onesrf_in=onesrf)
    in_maps = [dict(common, x_in=shards[c]) for c in range(8)]
    return in_maps


def kernel(**inputs):
    in_maps = _prep_host(**inputs)
    nc = build(NTILES)
    res = run_bass_kernel_spmd(nc, in_maps, core_ids=list(range(8)))
    outs = []
    for c in range(8):
        oc = res.results[c]["out"]
        outs.append(np.ascontiguousarray(
            oc.transpose(0, 3, 1, 2).reshape(2, 2048, 2048)))
    return np.concatenate(outs, axis=0)


# version 17 (final)
# speedup vs baseline: 1.1095x; 1.1095x over previous
import os as _os
import numpy as np
import ml_dtypes
import concourse.bass as bass
import concourse.mybir as mybir
from concourse.bass_utils import run_bass_kernel_spmd

F32 = mybir.dt.float32
BF16 = mybir.dt.bfloat16
AF = mybir.ActivationFunctionType
ALU = mybir.AluOpType
AX = mybir.AxisListType

D = 2048
NF = 16
TT = 512
NTB = 4
NTILES = 8
NE = 8
SCALE_ATT = float(128 ** (-0.5))
EPS_LN = 1e-5
EPS_ATT = 1e-9

SAME_ENGINE_SYNC = False
EXP_NH = int(_os.environ.get("EXP_NH", "16"))
SKIP_ROUTER = bool(int(_os.environ.get("SKIP_ROUTER", "0")))


class Prog:

    ENGS = ("pe", "dve", "act", "sp", "pool")
    DMA_SLOTS = {"sp": 48, "act": 16, "pool": 16}

    def __init__(self):
        self.items = {e: [] for e in self.ENGS}
        self.cnt = {e: 0 for e in self.ENGS}
        self.dma_cnt = {}
        self.dma_rr = {q: 0 for q in self.DMA_SLOTS}
        for q, n in self.DMA_SLOTS.items():
            for k in range(n):
                self.dma_cnt[f"{q}_d{k}"] = 0
        self.know = {e: {} for e in self.ENGS}
        self.snap = {}
        self.last_w = {}
        self.readers = {}
        self.max_wait = {}

    SELF_SYNC = ("dve", "act", "pool")

    def _wait(self, eng, sem, val):
        if sem == eng and eng not in self.SELF_SYNC:
            return
        k = self.know[eng]
        if k.get(sem, 0) >= val:
            return
        if sem in self.cnt:
            assert val <= self.cnt[sem], (
                f"wait on {sem}>={val} but only {self.cnt[sem]} incs emitted "
                f"(consumer of a non-inc'd op before its group's inc)")
        self.items[eng].append(("wait", sem, val))
        if sem == "pe" and eng in ("dve", "act"):
            self.items[eng].append(("spacer",))
        self.max_wait[sem] = max(self.max_wait.get(sem, 0), val)
        sn = self.snap.get((sem, val))
        if sn:
            for s2, v2 in sn.items():
                if k.get(s2, 0) < v2:
                    k[s2] = v2
        k[sem] = val

    FULL_SERIAL = bool(int(_os.environ.get("FULL_SERIAL", "0")))

    def op(self, eng, fn, reads=(), writes=(), inc=True, dma=False):
        assert eng in self.ENGS
        if self.FULL_SERIAL:
            for s_ in self.ENGS:
                if s_ != eng and self.cnt[s_] > 0:
                    self._wait(eng, s_, self.cnt[s_])
            for s_, v_ in self.dma_cnt.items():
                if v_ > 0:
                    self._wait(eng, s_, v_)
        if dma:
            k = self.dma_rr[eng] % self.DMA_SLOTS[eng]
            self.dma_rr[eng] += 1
            sem = f"{eng}_d{k}"
            self.dma_cnt[sem] += 16
            ident = (sem, self.dma_cnt[sem])
        else:
            sem = eng
            if inc:
                self.cnt[eng] += 1
                ident = (eng, self.cnt[eng])
            else:
                ident = (eng, self.cnt[eng] + 1)
        for u_ in reads:
            lw = self.last_w.get(u_)
            if lw is not None:
                self._wait(eng, lw[0], lw[1])
            if u_.startswith("ps"):
                for s, v in self.readers.get(u_, {}).items():
                    self._wait(eng, s, v)
        for u_ in writes:
            lw = self.last_w.get(u_)
            if lw is not None:
                self._wait(eng, lw[0], lw[1])
            for s, v in self.readers.get(u_, {}).items():
                self._wait(eng, s, v)
        self.snap[ident] = dict(self.know[eng])
        for u_ in writes:
            self.last_w[u_] = ident
            self.readers[u_] = {}
        for u_ in reads:
            rd = self.readers.setdefault(u_, {})
            if rd.get(ident[0], 0) < ident[1]:
                rd[ident[0]] = ident[1]
        if dma:
            self.items[eng].append(("op", fn, sem, 16))
        else:
            self.items[eng].append(("op", fn, sem, 1 if inc else None))

    def barrier_op(self, eng, fn):
        self.items[eng].append(("op", fn, None, None))

    def finalize_check(self):
        for sem, v in self.max_wait.items():
            have = self.dma_cnt.get(sem, self.cnt.get(sem, 0))
            assert have >= v, f"deadlock: sem {sem} max wait {v} > total {have}"

    import os
    PE_DRAIN_NOP = int(os.environ.get("PE_DRAIN_NOP", "160"))

    def emit(self, eng_name, eng_obj, sems, spacer=None):
        for it in self.items[eng_name]:
            if it[0] == "spacer":
                if spacer is not None:
                    spacer(eng_obj)
                continue
            if it[0] == "wait":
                _, s, v = it
                eng_obj.wait_ge(sems[s], v)
            else:
                _, fn, sem, inc = it
                inst = fn(eng_obj)
                if sem is not None and inc is not None:
                    inst.then_inc(sems[sem], inc)


def build(n_tiles=NTILES, taps=()):
    taps = set(taps)
    nc = bass.Bass("TRN2", target_bir_lowering=False)

    x_in = nc.dram_tensor("x_in", [2, NF, 128, D], F32, kind="ExternalInput").ap()
    hq = nc.dram_tensor("hq", [48, 128, 16, 128], F32, kind="ExternalInput").ap()
    hw = nc.dram_tensor("hw", [16, 128, 16, 128], F32, kind="ExternalInput").ap()
    ho = nc.dram_tensor("ho", [16, 128, 16, 128], F32, kind="ExternalInput").ap()
    he = nc.dram_tensor("he", [8, 2, 2, 128, 8192], F32, kind="ExternalInput").ap()
    rw_in = nc.dram_tensor("rw_in", [128, 128], F32, kind="ExternalInput").ap()
    rb_in = nc.dram_tensor("rb_in", [1, 8], F32, kind="ExternalInput").ap()
    eb_in = nc.dram_tensor("eb_in", [1, 16384], F32, kind="ExternalInput").ap()
    ob_in = nc.dram_tensor("ob_in", [128, 16], F32, kind="ExternalInput").ap()
    idn_in = nc.dram_tensor("idn_in", [128, 128], BF16, kind="ExternalInput").ap()
    onesc_in = nc.dram_tensor("onesc_in", [128, 1], BF16, kind="ExternalInput").ap()
    onesr_in = nc.dram_tensor("onesr_in", [1, 128], BF16, kind="ExternalInput").ap()
    onesrf_in = nc.dram_tensor("onesrf_in", [1, 128], F32, kind="ExternalInput").ap()
    out = nc.dram_tensor("out", [2, NF, 128, D], F32, kind="ExternalOutput").ap()
    dbg = {}
    for t_ in taps:
        dt_ = F32 if t_ in ("x_att", "gates", "lgts") else BF16
        cols_ = {"gates": 32, "lgts": 32, "h0": 2048, "z0": 1024, "z1": 1024, "eb0": 2048}.get(t_, NF * TT)
        if t_ in ("z0", "z1"): dt_ = F32
        dbg[t_] = nc.dram_tensor(f"dbg_{t_}", [128, cols_], dt_, kind="ExternalOutput").ap()
    sq = nc.dram_tensor("sq", [48, 128, 2048], BF16, kind="Internal").ap()
    sw = nc.dram_tensor("sw", [16, 128, 2048], BF16, kind="Internal").ap()
    so = nc.dram_tensor("so", [16, 128, 2048], BF16, kind="Internal").ap()
    se = nc.dram_tensor("se", [8, 2, 2, 128, 8192], BF16, kind="Internal").ap()
    seb = nc.dram_tensor("seb", [1, 16384], BF16, kind="Internal").ap()

    p = Prog()

    from contextlib import ExitStack
    with ExitStack() as _st:
        xb = _st.enter_context(nc.sbuf_tensor("xb", [128, NF * TT], F32))
        yb = _st.enter_context(nc.sbuf_tensor("yb", [128, NF * TT], BF16))
        kb = _st.enter_context(nc.sbuf_tensor("kb", [128, NF * TT], BF16))
        vb = _st.enter_context(nc.sbuf_tensor("vb", [128, NF * TT], BF16))
        kvcb = _st.enter_context(nc.sbuf_tensor("kvcb", [128, NF * TT], BF16))
        kcb = _st.enter_context(nc.sbuf_tensor("kcb", [128, NF * TT], BF16))
        wq0 = _st.enter_context(nc.sbuf_tensor("wq0", [128, 2048], BF16))
        wq1 = _st.enter_context(nc.sbuf_tensor("wq1", [128, 2048], BF16))
        wq2 = _st.enter_context(nc.sbuf_tensor("wq2", [128, 2048], BF16))
        wq3 = _st.enter_context(nc.sbuf_tensor("wq3", [128, 2048], BF16))
        we0 = _st.enter_context(nc.sbuf_tensor("we0", [128, 8192], BF16))
        we1 = _st.enter_context(nc.sbuf_tensor("we1", [128, 8192], BF16))
        we2 = _st.enter_context(nc.sbuf_tensor("we2", [128, 8192], BF16))
        kvsc = _st.enter_context(nc.sbuf_tensor("kvsc", [128, 2 * TT], BF16))
        fsc = _st.enter_context(nc.sbuf_tensor("fsc", [128, 3 * TT], F32))
        bsc = _st.enter_context(nc.sbuf_tensor("bsc", [128, 6 * TT], BF16))
        hsc = _st.enter_context(nc.sbuf_tensor("hsc", [128, 2 * 1024], BF16))
        rows = _st.enter_context(nc.sbuf_tensor("rows", [1, 1536], F32))
        kvst = _st.enter_context(nc.sbuf_tensor("kvst", [128, NF], F32))
        kcst = _st.enter_context(nc.sbuf_tensor("kcst", [128, NF], F32))
        gts = _st.enter_context(nc.sbuf_tensor("gts", [128, 32], F32))
        lgb = _st.enter_context(nc.sbuf_tensor("lgb", [128, 32], F32)) if "lgts" in taps else None
        rsc = _st.enter_context(nc.sbuf_tensor("rsc", [128, 40], F32))
        consts = _st.enter_context(nc.sbuf_tensor("consts", [128, 4], F32))
        junk = _st.enter_context(nc.sbuf_tensor("junk", [128, 256], F32))
        rwb = _st.enter_context(nc.sbuf_tensor("rwb", [128, 128], BF16))
        rbb = _st.enter_context(nc.sbuf_tensor("rbb", [1, 8], BF16))
        ebs = _st.enter_context(nc.sbuf_tensor("ebs", [1, 2048], BF16))
        obb = _st.enter_context(nc.sbuf_tensor("obb", [128, 16], F32))
        idn = _st.enter_context(nc.sbuf_tensor("idn", [128, 128], BF16))
        onesc = _st.enter_context(nc.sbuf_tensor("onesc", [128, 1], BF16))
        onesr = _st.enter_context(nc.sbuf_tensor("onesr", [1, 128], BF16))
        onesrf = _st.enter_context(nc.sbuf_tensor("onesrf", [1, 128], F32))
        psA0 = _st.enter_context(nc.psum_tensor("psA0", [128, 512], F32))
        psA1 = _st.enter_context(nc.psum_tensor("psA1", [128, 512], F32))
        psA2 = _st.enter_context(nc.psum_tensor("psA2", [128, 512], F32))
        psA3 = _st.enter_context(nc.psum_tensor("psA3", [128, 512], F32))
        psB0 = _st.enter_context(nc.psum_tensor("psB0", [128, 512], F32))
        psB1 = _st.enter_context(nc.psum_tensor("psB1", [128, 512], F32))
        psB2 = _st.enter_context(nc.psum_tensor("psB2", [128, 512], F32))
        psB3 = _st.enter_context(nc.psum_tensor("psB3", [128, 512], F32))
        sem_pe = _st.enter_context(nc.semaphore("pe"))
        sem_dve = _st.enter_context(nc.semaphore("dve"))
        sem_act = _st.enter_context(nc.semaphore("act"))
        sems = {"pe": sem_pe, "dve": sem_dve, "act": sem_act}
        for q_, n_ in Prog.DMA_SLOTS.items():
            for k_ in range(n_):
                nm = f"{q_}_d{k_}"
                sems[nm] = _st.enter_context(nc.semaphore(nm))
        block = _st.enter_context(nc.Block())
        nums = sorted(s.num for s in sems.values())
        assert nums == list(range(nums[0], nums[0] + len(nums))), nums
        srange = range(nums[0], nums[-1] + 1)
        nc.gpsimd.dma_reset(srange)
        nc.gpsimd.sem_clear(srange)
        nc.all_engine_barrier()
        psA = [psA0, psA1, psA2, psA3]
        psB = [psB0, psB1, psB2, psB3]
        wq = [wq0, wq1, wq2, wq3]
        we = [we0, we1, we2]

        def u(buf, i, w=TT):
            return buf[:, i * w:(i + 1) * w]

        p.op("dve", lambda v: v.memset(consts[:, 0:1], SCALE_ATT), writes=["c0"])
        p.op("dve", lambda v: v.memset(consts[:, 1:2], EPS_LN), writes=["c1"])
        p.op("dve", lambda v: v.memset(consts[:, 2:3], EPS_ATT), writes=["c2"])
        c_att = consts[:, 0:1]
        c_eps5 = consts[:, 1:2]
        c_eps9 = consts[:, 2:3]
        p.op("sp", lambda s: s.dma_start(idn[:, :], idn_in), writes=["idn"], dma=True)
        p.op("sp", lambda s: s.dma_start(onesc[:, :], onesc_in), writes=["onesc"], dma=True)
        p.op("sp", lambda s: s.dma_start(onesr[:, :], onesr_in), writes=["onesr"], dma=True)
        p.op("sp", lambda s: s.dma_start(onesrf[:, :], onesrf_in), writes=["onesrf"], dma=True)
        p.op("pool", lambda g: g.dma_start(rwb[:, :], rw_in), writes=["rw"], dma=True)
        p.op("pool", lambda g: g.dma_start(rbb[:, :], rb_in), writes=["rb"], dma=True)
        p.op("sp", lambda s: s.dma_start(obb[:, :], ob_in), writes=["ob"], dma=True)

        wq_state = {}

        def load_wq_slab(tile_i, kind, idx, slot):
            dst = wq[slot]
            key = f"wqs:{slot}"
            if kind == "q":
                src_f, src_b, skey = hq[idx], sq[idx], f"sq:{idx}"
            elif kind == "w":
                src_f, src_b, skey = hw[idx], sw[idx], f"sw:{idx}"
            else:
                src_f, src_b, skey = ho[idx], so[idx], f"so:{idx}"
            if tile_i == 0:
                sf = src_f.rearrange("p a b -> p (a b)")
                p.op("pool", lambda g, d=dst, s=sf: g.dma_start(d[:, :], s),
                     writes=[key], dma=True)
                p.op("act", lambda a, d=dst, s=src_b: a.dma_start(s, d[:, :]),
                     reads=[key], writes=[skey], dma=True)
            else:
                p.op("sp", lambda s_, d=dst, s=src_b: s_.dma_start(d[:, :], s),
                     reads=[skey], writes=[key], dma=True)
            return dst, key

        WE_SLOTS = [
            (we0, ["wes:0"]),
            (we1, ["wes:1"]),
            (we2, ["wes:2"]),
            (kcb, [f"kc:{u_}" for u_ in range(NF)]),
            (vb, [f"v:{u_}" for u_ in range(NF)]),
        ]

        def load_we_slab(tile_i, e, dh, hh, slot):
            dst, keys = WE_SLOTS[slot]
            skey = f"se:{e}:{dh}:{hh}"
            if tile_i == 0:
                sf = he[e, dh, hh]
                p.op("pool", lambda g, d=dst, s=sf: g.dma_start(d[:, :], s),
                     writes=keys, dma=True)
                p.op("act", lambda a, d=dst, s=se[e, dh, hh]: a.dma_start(s, d[:, :]),
                     reads=keys, writes=[skey], dma=True)
            else:
                p.op("sp", lambda s_, d=dst, s=se[e, dh, hh]: s_.dma_start(d[:, :], s),
                     reads=[skey], writes=keys, dma=True)
            return dst, keys

        def layernorm(t_i, tag, post_chunk=None):
            xkeys = [f"x:{fc}" for fc in range(NF)]
            for fc in range(NF):
                sb = u(bsc, fc % 2)
                sq_ = u(bsc, 2 + fc % 2)
                p.op("dve", lambda v, a=sb, b=u(xb, fc): v.tensor_copy(a, b),
                     reads=[f"x:{fc}"], writes=[f"bsc:{fc % 2}"])
                p.op("act", lambda a_, a=sq_, b=u(xb, fc): a_.activation(a, b, AF.Square),
                     reads=[f"x:{fc}"], writes=[f"bsc:{2 + fc % 2}"])
                p.op("pe", lambda t, a=sb, fc=fc: t.matmul(
                    psA0[0:1, :], onesc[:, :], a, start=(fc == 0), stop=(fc == NF - 1)),
                    reads=[f"bsc:{fc % 2}", "onesc"], writes=["psA0"])
                p.op("pe", lambda t, a=sq_, fc=fc: t.matmul(
                    psA1[0:1, :], onesc[:, :], a, start=(fc == 0), stop=(fc == NF - 1)),
                    reads=[f"bsc:{2 + fc % 2}", "onesc"], writes=["psA1"])
            m_r = rows[0:1, 0:512]
            r_r = rows[0:1, 512:1024]
            t_r = rows[0:1, 1024:1536]
            p.op("dve", lambda v: v.tensor_scalar(m_r, psA0[0:1, :], 1.0 / D, None, ALU.mult),
                 reads=["psA0"], writes=["mr"])
            p.op("dve", lambda v: v.tensor_mul(t_r, m_r, m_r), reads=["mr"], writes=["tr"])
            p.op("dve", lambda v: v.scalar_tensor_tensor(
                t_r, psA1[0:1, :], 1.0 / D, t_r, ALU.mult, ALU.subtract),
                reads=["psA1", "tr"], writes=["tr"])
            p.op("act", lambda a: a.activation(t_r, t_r, AF.Ln, bias=c_eps5[0:1, 0:1]),
                 reads=["tr", "c1"], writes=["tr"])
            p.op("act", lambda a: a.activation(r_r, t_r, AF.Exp, scale=-0.5),
                 reads=["tr"], writes=["rr"])
            p.op("pe", lambda t: t.matmul(psA2[:, :], onesrf[:, :], m_r, start=True, stop=True),
                 reads=["mr", "onesrf"], writes=["psA2"])
            p.op("pe", lambda t: t.matmul(psA3[:, :], onesrf[:, :], r_r, start=True, stop=True),
                 reads=["rr", "onesrf"], writes=["psA3"])
            for fc in range(NF):
                sf = u(fsc, fc % 2)
                p.op("dve", lambda v, a=sf, b=u(xb, fc): v.tensor_sub(a, b, psA2[:, :]),
                     reads=[f"x:{fc}", "psA2"], writes=[f"fsc:{fc % 2}"])
                p.op("dve", lambda v, a=u(yb, fc), b=sf: v.tensor_mul(a, b, psA3[:, :]),
                     reads=[f"fsc:{fc % 2}", "psA3"], writes=[f"y:{fc}"])
                if post_chunk is not None:
                    post_chunk(fc)


        def tap(name, buf, i):
            if i == 0 and name in dbg:
                keys = [f"{pfx}:{u_}" for pfx in [ {"y1":"y","y2":"y","k":"k","v":"v",
                        "kc":"kc","kvc":"kvc","attn":"v","comb_tm":"kvc","comb_fm":"k"}.get(name, name) ]
                        for u_ in range(NF)]
                if name == "gates":
                    keys = [f"g:{tb_}" for tb_ in range(NTB)]
                elif name == "x_att":
                    keys = [f"x:{u_}" for u_ in range(NF)]
                p.op("sp", lambda s, b_=buf, d_=dbg[name]: s.dma_start(d_, b_[:, :]),
                     reads=keys, dma=True)

        for i in range(n_tiles):
            b, l0 = i // 4, (i % 4) * TT
            seq_start = (i % 4 == 0)

            for fc in range(NF):
                p.op("pool", lambda g, fc=fc, b=b, l0=l0: g.dma_start(
                    u(xb, fc), x_in[b, fc, :, l0:l0 + TT]),
                    writes=[f"x:{fc}"], dma=True)

            if seq_start:
                p.op("dve", lambda v: v.memset(kvst[:, :], 0.0), writes=["kvst"])
                p.op("dve", lambda v: v.memset(kcst[:, :], 0.0), writes=["kcst"])

            layernorm(i, "ln1")
            ykeys = [f"y:{fc}" for fc in range(NF)]

            tap("y1", yb, i)

            def proj_chunk(oc, slot, bank, stop_inc=True):
                dst, wkey = load_wq_slab(i, "q", oc, slot)
                for fc in range(NF):
                    p.op("pe", lambda t, w_=dst, fc=fc, bank=bank: t.matmul(
                        psA[bank][:, :], w_[:, fc * 128:(fc + 1) * 128], u(yb, fc),
                        start=(fc == 0), stop=(fc == NF - 1)),
                        reads=[wkey, f"y:{fc}"], writes=[f"psA{bank}"],
                        inc=(fc == NF - 1))

            def phi_from_psum(bank, out_ap, out_key, sidx):
                mn = u(fsc, 2)
                ex = u(bsc, 4 + sidx % 2)
                p.op("act", lambda a_, a=mn, bank=bank: a_.activation(
                    a, psA[bank][:, :], AF.Relu, scale=-1.0),
                    reads=[f"psA{bank}"], writes=["fsc:2"])
                p.op("act", lambda a_, a=ex, b=mn: a_.activation(a, b, AF.Exp, scale=-1.0),
                     reads=["fsc:2"], writes=[f"bsc:{4 + sidx % 2}"])
                p.op("dve", lambda v, o=out_ap, e=ex, bank=bank: v.scalar_tensor_tensor(
                    o, psA[bank][:, :], 0.0, e, ALU.max, ALU.add),
                    reads=[f"psA{bank}", f"bsc:{4 + sidx % 2}"], writes=[out_key])

            for j in range(NF):
                oc = 16 + j
                bank = j % 4
                proj_chunk(oc, j % 4, bank)
                phi_from_psum(bank, u(kb, j), f"k:{j}", j)
                p.op("dve", lambda v, j=j: v.tensor_tensor_scan(
                    u(kcb, j), u(kb, j), u(kb, j), kcst[:, j:j + 1], ALU.add, ALU.bypass),
                    reads=[f"k:{j}", "kcst"], writes=[f"kc:{j}"])
                p.op("dve", lambda v, j=j: v.tensor_copy(
                    kcst[:, j:j + 1], kcb[:, j * TT + TT - 1:j * TT + TT]),
                    reads=[f"kc:{j}"], writes=["kcst"])
            for j in range(NF):
                oc = 32 + j
                bank = j % 4
                proj_chunk(oc, j % 4, bank)
                p.op("dve", lambda v, j=j, bank=bank: v.tensor_copy(
                    u(vb, j), psA[bank][:, :]),
                    reads=[f"psA{bank}"], writes=[f"v:{j}"])
                kvs = u(kvsc, j % 2)
                p.op("dve", lambda v, a=kvs, j=j: v.tensor_mul(a, u(kb, j), u(vb, j)),
                     reads=[f"k:{j}", f"v:{j}"], writes=[f"kvsc:{j % 2}"])
                p.op("dve", lambda v, a=kvs, j=j: v.tensor_tensor_scan(
                    u(kvcb, j), a, a, kvst[:, j:j + 1], ALU.add, ALU.bypass),
                    reads=[f"kvsc:{j % 2}", "kvst"], writes=[f"kvc:{j}"])
                p.op("dve", lambda v, j=j: v.tensor_copy(
                    kvst[:, j:j + 1], kvcb[:, j * TT + TT - 1:j * TT + TT]),
                    reads=[f"kvc:{j}"], writes=["kvst"])
            for j in range(NF):
                oc = j
                bank = j % 4
                proj_chunk(oc, j % 4, bank)
                qap = u(bsc, j % 2)
                phi_from_psum(bank, qap, f"bsc:{j % 2}", j)
                den = u(fsc, j % 2)
                p.op("dve", lambda v, a=den, q=qap, j=j: v.tensor_mul(a, q, u(kcb, j)),
                     reads=[f"bsc:{j % 2}", f"kc:{j}"], writes=[f"fsc:{j % 2}"])
                p.op("act", lambda a_, a=den: a_.activation(a, a, AF.Ln, bias=c_eps9),
                     reads=[f"fsc:{j % 2}", "c2"], writes=[f"fsc:{j % 2}"])
                rcp = u(bsc, 2 + j % 2)
                p.op("act", lambda a_, a=rcp, b=den: a_.activation(a, b, AF.Exp, scale=-1.0),
                     reads=[f"fsc:{j % 2}"], writes=[f"bsc:{2 + j % 2}"])
                num = u(kvsc, j % 2)
                p.op("dve", lambda v, a=num, q=qap, j=j: v.tensor_mul(a, q, u(kvcb, j)),
                     reads=[f"bsc:{j % 2}", f"kvc:{j}"], writes=[f"kvsc:{j % 2}"])
                p.op("dve", lambda v, a=u(vb, j), n=num, r=rcp: v.tensor_mul(a, n, r),
                     reads=[f"kvsc:{j % 2}", f"bsc:{2 + j % 2}"], writes=[f"v:{j}"])

            tap("k", kb, i)
            tap("v", vb, i)
            tap("kc", kcb, i)
            tap("kvc", kvcb, i)
            tap("attn", vb, i)

            for fo in range(NF):
                bank = fo % 4
                dst, wkey = load_wq_slab(i, "w", fo, fo % 4)
                for dc in range(NF):
                    p.op("pe", lambda t, w_=dst, dc=dc, bank=bank: t.matmul(
                        psA[bank][:, :], w_[:, dc * 128:(dc + 1) * 128], u(vb, dc),
                        start=(dc == 0), stop=(dc == NF - 1)),
                        reads=[wkey, f"v:{dc}"], writes=[f"psA{bank}"],
                        inc=(dc == NF - 1))
                p.op("dve", lambda v, fo=fo, bank=bank: v.scalar_tensor_tensor(
                    u(xb, fo), psA[bank][:, :], c_att, u(xb, fo), ALU.mult, ALU.add),
                    reads=[f"psA{bank}", f"x:{fo}", "c0"], writes=[f"x:{fo}"])

            tap("x_att", xb, i)

            def router_chunk(fc):
                for tb in range(NTB):
                    lgp = psA0[:, tb * 8:(tb + 1) * 8]
                    p.op("pe", lambda t, fc=fc, tb=tb, lgp=lgp: t.matmul(
                        lgp, yb[:, fc * TT + tb * 128: fc * TT + tb * 128 + 128],
                        rwb[:, fc * 8:(fc + 1) * 8],
                        start=(fc == 0 and tb == 0), stop=False),
                        reads=[f"y:{fc}", "rw"], writes=["psA0"], inc=False)

            layernorm(i, "ln2", post_chunk=None if SKIP_ROUTER else router_chunk)

            if SKIP_ROUTER:
                p.op("dve", lambda v: v.memset(gts[:, :], 0.25),
                     writes=[f"g:{tb_}" for tb_ in range(NTB)])
            for tb in range(NTB if not SKIP_ROUTER else 0):
                lgp = psA0[:, tb * 8:(tb + 1) * 8]
                p.op("pe", lambda t, lgp=lgp: t.matmul(
                    lgp, onesr[:, :], rbb[:, :], start=False, stop=True),
                    reads=["onesr", "rb"], writes=["psA0"])
                lg = rsc[:, 0:8]
                m1 = rsc[:, 8:9]
                m2 = rsc[:, 9:10]
                msk = rsc[:, 16:24]
                eg = rsc[:, 24:32]
                lg2 = rsc[:, 32:40]
                p.op("dve", lambda v, lgp=lgp: v.tensor_copy(lg, lgp),
                     reads=["psA0"], writes=["rsc"])
                if lgb is not None:
                    p.op("dve", lambda v, tb=tb: v.tensor_copy(lgb[:, tb * 8:(tb + 1) * 8], lg),
                         reads=["rsc"], writes=["lgb"])
                p.op("dve", lambda v: v.reduce_max(m1, lg, axis=AX.X),
                     reads=["rsc"], writes=["rsc"])
                p.op("dve", lambda v: v.tensor_scalar(msk, lg, m1, -1e30, ALU.is_ge, ALU.mult),
                     reads=["rsc"], writes=["rsc"])
                p.op("dve", lambda v: v.tensor_add(lg2, lg, msk), reads=["rsc"], writes=["rsc"])
                p.op("dve", lambda v: v.reduce_max(m2, lg2, axis=AX.X),
                     reads=["rsc"], writes=["rsc"])
                p.op("dve", lambda v: v.tensor_scalar(msk, lg, m2, None, ALU.is_ge),
                     reads=["rsc"], writes=["rsc"])
                p.op("dve", lambda v: v.tensor_scalar(m1, m1, -1.0, None, ALU.mult),
                     reads=["rsc"], writes=["rsc"])
                p.op("act", lambda a: a.activation(eg, lg, AF.Exp, bias=m1[:, 0:1]),
                     reads=["rsc"], writes=["rsc"])
                p.op("dve", lambda v: v.tensor_mul(eg, eg, msk), reads=["rsc"], writes=["rsc"])
                p.op("dve", lambda v: v.reduce_sum(m2, eg, axis=AX.X),
                     reads=["rsc"], writes=["rsc"])
                p.op("dve", lambda v: v.reciprocal(m1, m2), reads=["rsc"], writes=["rsc"])
                p.op("dve", lambda v, tb=tb: v.tensor_scalar(
                    gts[:, tb * 8:(tb + 1) * 8], eg, m1, None, ALU.mult),
                    reads=["rsc"], writes=[f"g:{tb}"])

            tap("y2", yb, i)
            tap("gates", gts, i)
            if i == 0 and "lgts" in dbg:
                p.op("sp", lambda s: s.dma_start(dbg["lgts"], lgb[:, :]),
                     reads=["lgb"], dma=True)

            widx = 0
            for e in range(NE):
                if i == 0:
                    p.op("pool", lambda g, e=e: g.dma_start(
                        ebs[:, :], eb_in[0:1, e * 2048:(e + 1) * 2048]),
                        writes=["ebs"], dma=True)
                    p.op("act", lambda a, e=e: a.dma_start(
                        seb[0:1, e * 2048:(e + 1) * 2048], ebs[:, :]),
                        reads=["ebs"], writes=[f"seb:{e}"], dma=True)
                else:
                    p.op("sp", lambda s, e=e: s.dma_start(
                        ebs[:, :], seb[0:1, e * 2048:(e + 1) * 2048]),
                        reads=[f"seb:{e}"], writes=["ebs"], dma=True)
                for dh in range(2):
                    slabs = []
                    for hh in range(2):
                        dst, wkeys = load_we_slab(i, e, dh, hh, widx % 5)
                        widx += 1
                        slabs.append((dst, wkeys))
                    if i == 0 and e == 0 and dh == 0 and "w0" in dbg:
                        p.op("sp", lambda s: s.dma_start(dbg["w0"], we[0][:, :]),
                             reads=["wes:0"], dma=True)
                    for tb in range(NTB):
                        pb = (e * 8 + dh * 4 + tb) % 2
                        pkey = f"psB{pb}"
                        for hh in range(2):
                            dst, wkeys = slabs[hh]
                            for h8 in range(8):
                                h = hh * 8 + h8
                                if h >= EXP_NH:
                                    continue
                                first = (hh == 0 and h8 == 0)
                                for dg in range(2):
                                    p.op("pe", lambda t, d=dst, h=h, h8=h8, dg=dg, tb=tb, pb=pb, first=first: t.matmul(
                                        psB[pb * 2 + dg][:, :],
                                        yb[:, h * TT + tb * 128: h * TT + tb * 128 + 128],
                                        d[:, h8 * 1024 + dg * 512: h8 * 1024 + (dg + 1) * 512],
                                        start=first, stop=False),
                                        reads=wkeys + [f"y:{h}"], writes=[f"psB{pb * 2 + dg}"], inc=False)
                        for dg in range(2):
                            off = dh * 1024 + dg * 512
                            p.op("pe", lambda t, dg=dg, pb=pb, off=off: t.matmul(
                                psB[pb * 2 + dg][:, :], onesr[:, :],
                                ebs[0:1, off:off + 512], start=False, stop=True),
                                reads=["onesr", "ebs"], writes=[f"psB{pb * 2 + dg}"],
                                inc=(dg == 1))
                        if i == 0 and e == 0 and dh == 1 and tb == 0 and "z1" in dbg:
                            for dg in range(2):
                                p.op("dve", lambda v, dg=dg, pb=pb: v.tensor_copy(
                                    u(fsc, dg), psB[pb * 2 + dg][:, :]),
                                    reads=[f"psB{pb * 2 + dg}"], writes=[f"fsc:{dg}"])
                                p.op("sp", lambda s, dg=dg: s.dma_start(
                                    dbg["z1"][:, dg * 512:(dg + 1) * 512], u(fsc, dg)),
                                    reads=[f"fsc:{dg}"], dma=True)
                        if i == 0 and e == 0 and dh == 1 and "w0" in dbg and tb == 0:
                            p.op("sp", lambda s, sl=slabs[0][0]: s.dma_start(dbg["w0"], sl[:, :]),
                                 reads=[slabs[0][1]], dma=True)
                        if i == 0 and e == 0 and dh == 0 and tb == 0 and "z0" in dbg:
                            for dg in range(2):
                                p.op("dve", lambda v, dg=dg, pb=pb: v.tensor_copy(
                                    u(fsc, dg), psB[pb * 2 + dg][:, :]),
                                    reads=[f"psB{pb * 2 + dg}"], writes=[f"fsc:{dg}"])
                                p.op("sp", lambda s, dg=dg: s.dma_start(
                                    dbg["z0"][:, dg * 512:(dg + 1) * 512], u(fsc, dg)),
                                    reads=[f"fsc:{dg}"], dma=True)
                        if i == 0 and e == 0 and dh == 0 and tb == 0 and "eb0" in dbg:
                            p.op("sp", lambda s: s.dma_start(dbg["eb0"][0:1, 0:2048], ebs[:, :]),
                                 reads=["ebs"], dma=True)
                        hs = u(hsc, tb % 2, 1024)
                        for dg in range(2):
                            p.op("act", lambda a, h=hs, pb=pb, dg=dg: a.activation(
                                h[:, dg * 512:(dg + 1) * 512], psB[pb * 2 + dg][:, :],
                                AF.Gelu_apprx_tanh),
                                reads=[f"psB{pb * 2 + dg}"], writes=[f"hsc:{tb % 2}"])
                        if i == 0 and e == 0 and tb == 0 and "h0" in dbg:
                            p.op("sp", lambda s, h=hs, dh=dh: s.dma_start(
                                dbg["h0"][:, dh * 1024:(dh + 1) * 1024], h),
                                reads=[f"hsc:{tb % 2}"], dma=True)
                        cunits = [f"kvc:{tb * 4 + dh * 2}", f"kvc:{tb * 4 + dh * 2 + 1}"]
                        cap = kvcb[:, tb * 2048 + dh * 1024: tb * 2048 + (dh + 1) * 1024]
                        gcol = gts[:, tb * 8 + e:tb * 8 + e + 1]
                        if e == 0:
                            p.op("dve", lambda v, c=cap, h=hs, g=gcol: v.tensor_scalar(
                                c, h, g, None, ALU.mult),
                                reads=[f"hsc:{tb % 2}", f"g:{tb}"], writes=cunits)
                        else:
                            p.op("dve", lambda v, c=cap, h=hs, g=gcol: v.scalar_tensor_tensor(
                                c, h, g, c, ALU.mult, ALU.add),
                                reads=[f"hsc:{tb % 2}", f"g:{tb}"] + cunits, writes=cunits)

            tap("comb_tm", kvcb, i)

            for tb in range(NTB):
                pb = tb % 2
                cunits = [f"kvc:{tb * 4 + q_}" for q_ in range(4)]
                for half in range(2):
                    pbt = psB[pb * 2 + half]
                    pbf = pbt[:, :].bitcast(BF16)
                    for d8 in range(8):
                        dc = half * 8 + d8
                        p.op("pe", lambda t, d8=d8, dc=dc, tb=tb, pbf=pbf: t.transpose(
                            pbf[:, d8 * 128:(d8 + 1) * 128],
                            kvcb[:, tb * 2048 + dc * 128: tb * 2048 + dc * 128 + 128],
                            idn[:, :]),
                            reads=cunits + ["idn"], writes=[f"psB{pb * 2 + half}"],
                            inc=(d8 == 7))
                    dst = kb[:, :].rearrange("p (dc t) -> p dc t", dc=NF)[
                        :, half * 8:(half + 1) * 8, tb * 128:tb * 128 + 128]
                    src2 = pbf.rearrange("p (d8 t) -> p d8 t", d8=8)
                    p.op("dve", lambda v, d=dst, s=src2: v.tensor_copy(d, s),
                         reads=[f"psB{pb * 2 + half}"],
                         writes=[f"k:{half * 8 + d8_}" for d8_ in range(8)])

            tap("comb_fm", kb, i)

            for fo in range(NF):
                bank = fo % 4
                dst, wkey = load_wq_slab(i, "o", fo, fo % 4)
                for dc in range(NF):
                    p.op("pe", lambda t, w_=dst, dc=dc, bank=bank: t.matmul(
                        psA[bank][:, :], w_[:, dc * 128:(dc + 1) * 128], u(kb, dc),
                        start=(dc == 0), stop=(dc == NF - 1)),
                        reads=[wkey, f"k:{dc}"], writes=[f"psA{bank}"],
                        inc=(dc == NF - 1))
                p.op("dve", lambda v, fo=fo, bank=bank: v.scalar_tensor_tensor(
                    u(xb, fo), psA[bank][:, :], obb[:, fo:fo + 1], u(xb, fo),
                    ALU.add, ALU.add),
                    reads=[f"psA{bank}", f"x:{fo}", "ob"], writes=[f"x:{fo}"])
                p.op("pool", lambda g, fo=fo, b=b, l0=l0: g.dma_start(
                    out[b, fo, :, l0:l0 + TT], u(xb, fo)),
                    reads=[f"x:{fo}"], dma=True)

        p.finalize_check()
        fc_ = dict(p.dma_cnt)
        cn = dict(p.cnt)

        def emit_all():
            @block.sync
            def _(s):
                p.emit("sp", s, sems)
                for nm_, v_ in fc_.items():
                    if v_ > 0:
                        s.wait_ge(sems[nm_], v_)
                s.wait_ge(sem_pe, cn["pe"])
                s.wait_ge(sem_dve, cn["dve"])
                s.wait_ge(sem_act, cn["act"])

            @block.gpsimd
            def _(g):
                p.emit("pool", g, sems)

            @block.vector
            def _(v):
                p.emit("dve", v, sems,
                       spacer=lambda e: e.tensor_copy(junk[:, 0:128], junk[:, 128:256]))

            @block.scalar
            def _(a):
                p.emit("act", a, sems,
                       spacer=lambda e: e.copy(junk[:, 0:128], junk[:, 128:256]))

            @block.tensor
            def _(t):
                p.emit("pe", t, sems)

        emit_all()
    return nc


def _prep_host(L0_attn_x, L0_attn_W_qkv, L0_attn_W_out, L0_ffn_router_w,
               L0_ffn_router_b, L0_ffn_expert_w, L0_ffn_expert_b,
               L0_ffn_out_w, L0_ffn_out_b):
    bf = ml_dtypes.bfloat16
    f32 = np.float32
    W_qkv = np.asarray(L0_attn_W_qkv, f32)
    W_out = np.asarray(L0_attn_W_out, f32)
    out_w = np.asarray(L0_ffn_out_w, f32)
    exp_w = np.asarray(L0_ffn_expert_w, f32)

    hq = np.ascontiguousarray(W_qkv.reshape(48, 128, 16, 128).transpose(0, 3, 2, 1))
    hw = np.ascontiguousarray(W_out.reshape(16, 128, 16, 128).transpose(0, 3, 2, 1))
    ho = np.ascontiguousarray(out_w.reshape(16, 128, 16, 128).transpose(2, 1, 0, 3))
    he = np.ascontiguousarray(
        exp_w.reshape(8, 2, 8, 128, 2, 1024).transpose(0, 4, 1, 3, 2, 5)
        .reshape(8, 2, 2, 128, 8192))
    rw = np.ascontiguousarray(
        np.asarray(L0_ffn_router_w, f32).reshape(16, 128, 8).transpose(1, 0, 2)
        .reshape(128, 128))
    rb = np.asarray(L0_ffn_router_b, f32).reshape(1, 8)
    eb = np.asarray(L0_ffn_expert_b, f32).reshape(1, 16384)
    ob = np.ascontiguousarray(np.asarray(L0_ffn_out_b, f32).reshape(16, 128).T)
    idn = np.eye(128, dtype=bf)
    onesc = np.ones((128, 1), dtype=bf)
    onesr = np.ones((1, 128), dtype=bf)
    onesrf = np.ones((1, 128), dtype=f32)

    x = np.asarray(L0_attn_x, f32)
    shards = []
    for c in range(8):
        xc = x[2 * c:2 * c + 2]
        xt = np.ascontiguousarray(
            xc.reshape(2, 2048, 16, 128).transpose(0, 2, 3, 1))
        shards.append(xt)
    common = dict(hq=hq, hw=hw, ho=ho, he=he, rw_in=rw, rb_in=rb, eb_in=eb,
                  ob_in=ob, idn_in=idn, onesc_in=onesc, onesr_in=onesr,
                  onesrf_in=onesrf)
    in_maps = [dict(common, x_in=shards[c]) for c in range(8)]
    return in_maps


def kernel(**inputs):
    in_maps = _prep_host(**inputs)
    nc = build(NTILES)
    res = run_bass_kernel_spmd(nc, in_maps, core_ids=list(range(8)))
    outs = []
    for c in range(8):
        oc = res.results[c]["out"]
        outs.append(np.ascontiguousarray(
            oc.transpose(0, 3, 1, 2).reshape(2, 2048, 2048)))
    return np.concatenate(outs, axis=0)
